# version 01
# baseline (speedup 1.0000x reference)
import numpy as np
import concourse.bass as bass
import concourse.mybir as mybir
from concourse.bass import IndirectOffsetOnAxis
from concourse.expressions_rust import RuntimeValue
from concourse.bass_utils import run_bass_kernel_spmd

F32 = mybir.dt.float32
BF16 = mybir.dt.bfloat16
I32 = mybir.dt.int32
U32 = mybir.dt.uint32
AF = mybir.ActivationFunctionType
OP = mybir.AluOpType

GG = 2
T = 8192
D = 1024
NH = 16
HD = 64
E = 8
CAP = 640
FF = 4096
QP = 4
TLOC = T // QP
NTILE_LOC = TLOC // 128
NTILE = T // 128
EPS = 1e-5
BIGIDX = 60000.0


class Sem:

    def __init__(self, nc, name):
        self._cm = nc.semaphore(name)
        self.h = self._cm.__enter__()
        self.v = 0

    def tick(self, inst, n=1):
        inst.then_inc(self.h, n)
        self.v += n

    def dma(self, inst):
        self.tick(inst, 16)

    def wait(self, eng):
        if self.v > 0:
            eng.wait_ge(self.h, self.v)

    def wait_val(self, eng, val):
        if val > 0:
            eng.wait_ge(self.h, val)


class Prog:

    def __init__(self):
        self.nc = bass.Bass("TRN2", target_bir_lowering=False)
        self._ctx = []

    def sems(self):
        nc = self.nc
        for name in ("LD", "ST", "PE", "ACT", "DVE", "GP", "X1", "X2"):
            setattr(self, name, Sem(nc, name.lower()))

    def sbuf(self, name, shape, dtype):
        cm = self.nc.sbuf_tensor(name, shape, dtype)
        t = cm.__enter__()
        self._ctx.append(cm)
        return t

    def psum(self, name, shape, dtype=F32):
        cm = self.nc.psum_tensor(name, shape, dtype)
        t = cm.__enter__()
        self._ctx.append(cm)
        return t

    def block(self):
        cm = self.nc.Block()
        b = cm.__enter__()
        self._ctx.append(cm)
        return b

    def close(self):
        for cm in reversed(self._ctx):
            cm.__exit__(None, None, None)



def host_consts():
    c = {}
    c["ident"] = np.eye(128, dtype=np.float32)
    c["triu"] = np.triu(np.ones((128, 128), dtype=np.float32)).astype(np.float32)
    c["cols8"] = np.tile(np.arange(8, dtype=np.float32), (128, 1))
    c["triu_bf"] = np.triu(np.ones((128, 128), dtype=np.float32))
    c["striu_bf"] = np.triu(np.ones((128, 128), dtype=np.float32), 1)
    c["ones128x1_bf"] = np.ones((128, 1), dtype=np.float32)
    c["ones1x128"] = np.ones((1, 128), dtype=np.float32)
    c["eye64s_bf"] = np.vstack([np.eye(64, dtype=np.float32)] * 2)
    return c


def emit_ln_tile(p, xt, stat, mv, sd, rstd, lnf, lnb):
    nc = p.nc
    nc.vector.bn_stats(stat[:, 0:6], xt[:, 0:512])
    ins = nc.vector.bn_stats(stat[:, 6:12], xt[:, 512:1024])
    p.DVE.tick(ins)
    p.DVE.wait(nc.vector)
    ins = nc.vector.bn_aggr(mv[:], stat[:])
    p.DVE.tick(ins)
    p.DVE.wait(nc.vector)
    ins = nc.vector.tensor_scalar_add(sd[:], mv[:, 1:2], EPS)
    p.DVE.tick(ins)
    p.DVE.wait(nc.scalar)
    ins = nc.scalar.activation(sd[:], sd[:], AF.Sqrt)
    p.ACT.tick(ins)
    p.ACT.wait(nc.vector)
    ins = nc.vector.reciprocal(rstd[:], sd[:])
    p.DVE.tick(ins)
    p.DVE.wait(nc.vector)
    ins = nc.vector.tensor_scalar(lnf[:], xt[:], mv[:, 0:1], rstd[:, 0:1],
                                  OP.subtract, OP.mult)
    p.DVE.tick(ins)
    if lnb is not None:
        ins = nc.vector.tensor_copy(lnb[:], lnf[:])
        p.DVE.tick(ins)


def emit_router_tile(p, lnf, identsb, wr_sb, lnT, ptr, plog, logit, scr, dec):
    nc = p.nc
    for c in range(8):
        bank = ptr[c % 2]
        p.ACT.wait(nc.tensor)
        ins = nc.tensor.transpose(bank[:], lnf[:, c * 128:(c + 1) * 128], identsb[:])
        p.PE.tick(ins)
        p.PE.wait(nc.scalar)
        ins = nc.scalar.copy(lnT[:, c * 128:(c + 1) * 128], bank[:])
        p.ACT.tick(ins)
    p.ACT.wait(nc.tensor)
    for c in range(8):
        ins = nc.tensor.matmul(plog[:], lnT[:, c * 128:(c + 1) * 128],
                               wr_sb[:, c, :], start=(c == 0), stop=(c == 7))
    p.PE.tick(ins)
    p.PE.wait(nc.scalar)
    p.DVE.wait(nc.scalar)
    ins = nc.scalar.copy(logit[:], plog[:])
    p.ACT.tick(ins)
    p.ACT.wait(nc.vector)
    mx1 = scr["mx1"]
    ins = nc.vector.reduce_max(mx1[:], logit[:], axis=mybir.AxisListType.X)
    p.DVE.tick(ins)
    p.DVE.wait(nc.vector)
    ins = nc.vector.tensor_scalar(logit[:], logit[:], mx1[:, 0:1], None, OP.subtract)
    p.DVE.tick(ins)
    p.DVE.wait(nc.scalar)
    ins = nc.scalar.activation(logit[:], logit[:], AF.Exp)
    p.ACT.tick(ins)
    p.ACT.wait(nc.vector)
    sm = scr["sm"]
    ins = nc.vector.reduce_sum(sm[:], logit[:], axis=mybir.AxisListType.X)
    p.DVE.tick(ins)
    p.DVE.wait(nc.vector)
    ins = nc.vector.reciprocal(sm[:], sm[:])
    p.DVE.tick(ins)
    p.DVE.wait(nc.vector)
    ins = nc.vector.tensor_scalar(logit[:], logit[:], sm[:, 0:1], None, OP.mult)
    p.DVE.tick(ins)
    p.DVE.wait(nc.vector)
    mx8, mi8, mif = scr["mx8"], scr["mi8"], scr["mif"]
    ins = nc.vector.max(mx8[:], logit[:])
    p.DVE.tick(ins)
    p.DVE.wait(nc.vector)
    ins = nc.vector.max_index(mi8[:], mx8[:], logit[:])
    p.DVE.tick(ins)
    p.DVE.wait(nc.vector)
    ins = nc.vector.tensor_copy(dec[:, 0:2], mi8[:, 0:2].bitcast(I32))
    p.DVE.tick(ins)
    ins = nc.vector.tensor_copy(dec[:, 2:4], mx8[:, 0:2])
    p.DVE.tick(ins)




def emit_bookkeeping(p, decw_ap, scr, consts_sb, pA, pB, pC):
    nc = p.nc
    decsb = scr["decsb"]; masks = scr["masks"]; totrow = scr["totrow"]
    totv = scr["totv"]; totv_bf = scr["totv_bf"]; offs_sb = scr["offs_sb"]
    offsrow = scr["offsrow"]; prod = scr["prod"]; posl = scr["posl"]
    vflag = scr["vflag"]; esel = scr["esel"]; slots = scr["slots"]
    bk1 = scr["bk_scr1"]; bk2 = scr["bk_scr2"]
    cols8 = consts_sb["cols8"]; triu = consts_sb["triu_bf"]
    striu = consts_sb["striu_bf"]; ones128x1 = consts_sb["ones128x1_bf"]
    ones1x128 = consts_sb["ones1x128"]

    ins = nc.sync.dma_start(out=decsb[:], in_=decw_ap)
    p.LD.dma(ins)
    p.LD.wait(nc.vector)
    for j in range(NTILE):
        for k in range(2):
            ins = nc.vector.tensor_scalar(masks[:, j, k, :], cols8[:],
                                          decsb[:, j, k:k + 1], None, OP.is_equal)
    p.DVE.tick(ins)
    p.DVE.wait(nc.tensor)
    mflat = masks.rearrange("p j k e -> p (j k e)")
    nc.tensor.matmul(pA[:, 0:512], triu[:], mflat[:, 0:512], start=True, stop=True)
    nc.tensor.matmul(pA[:, 512:1024], triu[:], mflat[:, 512:1024], start=True, stop=True)
    nc.tensor.matmul(pB[0:1, 0:512], ones128x1[:], mflat[:, 0:512], start=True, stop=True)
    ins = nc.tensor.matmul(pB[0:1, 512:1024], ones128x1[:], mflat[:, 512:1024], start=True, stop=True)
    p.PE.tick(ins)
    p.PE.wait(nc.scalar)
    ins = nc.scalar.copy(totrow[0:1, :], pB[0:1, :])
    p.ACT.tick(ins)
    p.ACT.wait(nc.sync)
    ins = nc.sync.dma_start(out=bk1.rearrange("(p n) -> p n", p=1), in_=totrow[0:1, :])
    p.ST.dma(ins)
    p.ST.wait(nc.sync)
    for kk in range(2):
        ins = nc.sync.dma_start(
            out=totv[kk * NTILE:(kk + 1) * NTILE, :],
            in_=bk1.rearrange("(j k e) -> j k e", j=NTILE, k=2)[:, kk, :])
        p.LD.dma(ins)
    p.LD.wait(nc.vector)
    ins = nc.vector.tensor_copy(totv_bf[:], totv[:])
    p.DVE.tick(ins)
    p.DVE.wait(nc.tensor)
    ins = nc.tensor.matmul(pC[:, 0:8], striu[:], totv_bf[:], start=True, stop=True)
    p.PE.tick(ins)
    p.PE.wait(nc.scalar)
    ins = nc.scalar.copy(offs_sb[:], pC[:, 0:8])
    p.ACT.tick(ins)
    p.ACT.wait(nc.sync)
    ins = nc.sync.dma_start(out=bk2.rearrange("(v e) -> v e", v=128), in_=offs_sb[:])
    p.ST.dma(ins)
    p.ST.wait(nc.sync)
    ins = nc.sync.dma_start(
        out=offsrow[0:1, :].rearrange("p (j k e) -> p j k e", j=NTILE, k=2),
        in_=bk2.rearrange("(k j e) -> j k e", k=2, j=NTILE).unsqueeze(0))
    p.LD.dma(ins)
    p.LD.wait(nc.tensor)
    for j in range(NTILE):
        for k in range(2):
            col = j * 16 + k * 8
            ins = nc.tensor.matmul(pA[:, col:col + 8], ones1x128[0:1, :],
                                   offsrow[0:1, col:col + 8], start=False, stop=True)
    p.PE.tick(ins)
    p.PE.wait(nc.vector)
    ins = nc.vector.tensor_tensor(prod[:], pA[:, :], mflat[:, :], OP.mult)
    p.DVE.tick(ins)
    p.DVE.wait(nc.vector)
    ins = nc.vector.tensor_reduce(posl[:], prod.rearrange("p (c e) -> p c e", e=E),
                                  mybir.AxisListType.X, OP.add)
    p.DVE.tick(ins)
    p.DVE.wait(nc.vector)
    nc.vector.tensor_scalar(vflag[:], posl[:], float(CAP), None, OP.is_le)
    nc.vector.tensor_copy(esel[:].rearrange("p (j k) -> p j k", k=2), decsb[:, :, 0:2])
    ins = nc.vector.scalar_tensor_tensor(slots[:], esel[:], float(CAP), posl[:],
                                         OP.mult, OP.add)
    p.DVE.tick(ins)
    p.DVE.wait(nc.vector)
    ins = nc.vector.tensor_scalar_add(slots[:], slots[:], -1.0)
    p.DVE.tick(ins)


def bookkeeping_scratch(p):
    return {
        "decsb": p.sbuf("bk_decsb", [128, NTILE, 4], F32),
        "masks": p.sbuf("bk_masks", [128, NTILE, 2, E], BF16),
        "totrow": p.sbuf("bk_totrow", [1, 1024], F32),
        "totv": p.sbuf("bk_totv", [128, E], F32),
        "totv_bf": p.sbuf("bk_totv_bf", [128, E], BF16),
        "offs_sb": p.sbuf("bk_offs", [128, E], F32),
        "offsrow": p.sbuf("bk_offsrow", [1, 1024], F32),
        "prod": p.sbuf("bk_prod", [128, 1024], F32),
        "posl": p.sbuf("bk_posl", [128, 128], F32),
        "vflag": p.sbuf("bk_vflag", [128, 128], F32),
        "esel": p.sbuf("bk_esel", [128, 128], F32),
        "slots": p.sbuf("bk_slots", [128, 128], F32),
        "bk_scr1": p.nc.dram_tensor("bk_scr1", [1024], F32, kind="Internal").ap(),
        "bk_scr2": p.nc.dram_tensor("bk_scr2", [1024], F32, kind="Internal").ap(),
    }


def load_consts(p, names):
    out = {}
    nc = p.nc
    for n in names:
        dt = BF16 if n.endswith("_bf") else F32
        shp = {"ident": [128, 128], "triu_bf": [128, 128], "striu_bf": [128, 128],
               "cols8": [128, 8], "ones128x1_bf": [128, 1], "ones1x128": [1, 128],
               "eye64s_bf": [128, 64]}[n]
        ap = nc.dram_tensor("c_" + n, shp, F32, kind="ExternalInput").ap()
        sb = p.sbuf("cs_" + n, shp, dt)
        if dt == F32:
            ins = nc.sync.dma_start(out=sb[:], in_=ap)
        else:
            ins = nc.gpsimd.dma_start(out=sb[:], in_=ap)
        p.LD.dma(ins)
        out[n] = sb
    return out



def build_B():
    p = Prog()
    nc = p.nc
    lnb = nc.dram_tensor("lnb", [T, D], BF16, kind="ExternalInput").ap()
    decw = nc.dram_tensor("decw", [128, NTILE, 4], F32, kind="ExternalInput").ap()
    wi2 = nc.dram_tensor("wi2", [2, D, FF], F32, kind="ExternalInput").ap()
    wo2 = nc.dram_tensor("wo2", [2, FF, D], F32, kind="ExternalInput").ap()
    eq0_i = nc.dram_tensor("eq0", [128, 2], F32, kind="ExternalInput").ap()
    etab_o = nc.dram_tensor("etab", [2 * CAP, D], BF16, kind="ExternalOutput").ap()
    table = nc.dram_tensor("dtab", [2 * CAP, D], BF16, kind="Internal").ap()

    scr = bookkeeping_scratch(p)
    eq0 = p.sbuf("eq0_sb", [128, 2], F32)
    ma = p.sbuf("ma_sb", [128, 128], F32)
    mb = p.sbuf("mb_sb", [128, 128], F32)
    sidx_f = p.sbuf("sidx_f", [128, 128], F32)
    sidx_i = p.sbuf("sidx_i", [128, 128], I32)
    lnt = [p.sbuf("lnt0", [128, D], BF16), p.sbuf("lnt1", [128, D], BF16)]
    xe = p.sbuf("xe", [128, 8, 2 * CAP], BF16)
    wib = [p.sbuf("wib0", [128, 8, 128], BF16), p.sbuf("wib1", [128, 8, 128], BF16)]
    h = p.sbuf("h", [128, 32, CAP], BF16)
    wob = p.sbuf("wob", [128, 32, D], BF16)
    eob = p.sbuf("eob", [128, D], BF16)

    pA = p.psum("pA", [128, 1024], F32)
    pB = p.psum("pB", [128, 1024], F32)
    pC = p.psum("pC", [128, 512], F32)
    pD = p.psum("pD", [128, 512], F32)

    p.block()
    p.sems()

    consts_sb = load_consts(p, ["cols8", "triu_bf", "striu_bf", "ones128x1_bf",
                                "ones1x128"])
    ins = nc.sync.dma_start(out=eq0[:], in_=eq0_i[:])
    p.LD.dma(ins)
    emit_bookkeeping(p, decw[:], scr, consts_sb, pA, pB, pC)

    vflag, slots_g, esel, posl = scr["vflag"], scr["slots"], scr["esel"], scr["posl"]
    p.DVE.wait(nc.vector)
    nc.vector.tensor_scalar(ma[:], esel[:], eq0[:, 0:1], None, OP.is_ge)
    nc.vector.tensor_scalar(mb[:], esel[:], eq0[:, 1:2], None, OP.is_le)
    ins = nc.vector.tensor_tensor(ma[:], ma[:], mb[:], OP.mult)
    p.DVE.tick(ins)
    p.DVE.wait(nc.vector)
    ins = nc.vector.tensor_tensor(ma[:], ma[:], vflag[:], OP.mult)
    p.DVE.tick(ins)
    nc.vector.tensor_scalar(sidx_f[:], esel[:], eq0[:, 0:1], None, OP.subtract)
    ins = nc.vector.scalar_tensor_tensor(sidx_f[:], sidx_f[:], float(CAP), posl[:],
                                         OP.mult, OP.add)
    p.DVE.tick(ins)
    p.DVE.wait(nc.vector)
    ins = nc.vector.tensor_scalar_add(sidx_f[:], sidx_f[:], -1.0 - BIGIDX)
    p.DVE.tick(ins)
    p.DVE.wait(nc.vector)
    ins = nc.vector.tensor_tensor(sidx_f[:], sidx_f[:], ma[:], OP.mult)
    p.DVE.tick(ins)
    p.DVE.wait(nc.vector)
    ins = nc.vector.tensor_scalar_add(sidx_f[:], sidx_f[:], BIGIDX)
    p.DVE.tick(ins)
    p.DVE.wait(nc.vector)
    ins = nc.vector.tensor_copy(sidx_i[:], sidx_f[:])
    p.DVE.tick(ins)
    p.DVE.wait(nc.gpsimd)
    p.DVE.wait(nc.sync)

    bcreg = nc.gpsimd.alloc_register("bcreg")
    nc.gpsimd.reg_mov(bcreg, 2 * CAP - 1)
    for j in range(NTILE):
        buf = lnt[j % 2]
        if j >= 2:
            p.GP.wait(nc.sync)
        ins = nc.sync.dma_start(out=buf[:], in_=lnb[j * 128:(j + 1) * 128, :])
        p.LD.dma(ins)
        p.LD.wait(nc.gpsimd)
        for k in range(2):
            col = j * 2 + k
            ins = nc.gpsimd.indirect_dma_start(
                out=table[:, :],
                out_offset=IndirectOffsetOnAxis(sidx_i[:, col:col + 1], axis=0),
                in_=buf[:, :], in_offset=None,
                bounds_check=bcreg, oob_is_err=False)
            p.GP.dma(ins)

    p.GP.wait(nc.sync)
    for c in range(8):
        ins = nc.sync.dma_start_transpose(xe[:, c, :],
                                          table[0:2 * CAP, c * 128:(c + 1) * 128])
        p.LD.dma(ins)
    p.LD.wait(nc.tensor)
    p.LD.wait(nc.gpsimd)

    ins = nc.gpsimd.dma_start(out=wob[:],
                              in_=wo2[0].rearrange("(kc p) n -> p kc n", p=128))
    p.GP.dma(ins)

    for ei in range(2):
        for fc in range(32):
            wbuf = wib[fc % 2]
            p.PE.wait(nc.gpsimd)
            ins = nc.gpsimd.dma_start(
                out=wbuf[:], in_=wi2[ei, :, fc * 128:(fc + 1) * 128]
                .rearrange("(kc p) m -> p kc m", p=128))
            p.GP.dma(ins)
            p.GP.wait(nc.tensor)
            ps = [pA, pB][fc % 2]
            p.ACT.wait(nc.tensor)
            for (n0, nw) in ((0, 512), (512, 128)):
                for kc in range(8):
                    ins = nc.tensor.matmul(
                        ps[:, n0:n0 + nw], wbuf[:, kc, :],
                        xe[:, kc, ei * CAP + n0: ei * CAP + n0 + nw],
                        start=(kc == 0), stop=(kc == 7))
            p.PE.tick(ins)
            p.PE.wait(nc.scalar)
            ins = nc.scalar.activation(h[:, fc, :], ps[:, 0:CAP], AF.Relu)
            p.ACT.tick(ins)
        if ei == 1:
            p.PE.wait(nc.gpsimd)
            ins = nc.gpsimd.dma_start(out=wob[:],
                                      in_=wo2[1].rearrange("(kc p) n -> p kc n", p=128))
            p.GP.dma(ins)
        p.GP.wait(nc.tensor)
        p.ACT.wait(nc.tensor)
        for s5 in range(5):
            for n2 in range(2):
                ps2 = [pC, pD][(s5 * 2 + n2) % 2]
                p.ACT.wait(nc.tensor)
                for kc2 in range(32):
                    ins = nc.tensor.matmul(
                        ps2[:, 0:512],
                        h[:, kc2, s5 * 128:(s5 + 1) * 128],
                        wob[:, kc2, n2 * 512:(n2 + 1) * 512],
                        start=(kc2 == 0), stop=(kc2 == 31))
                p.PE.tick(ins)
                p.PE.wait(nc.scalar)
                p.ST.wait(nc.scalar)
                ins = nc.scalar.copy(eob[:, n2 * 512:(n2 + 1) * 512], ps2[:, 0:512])
                p.ACT.tick(ins)
            p.ACT.wait(nc.sync)
            ins = nc.sync.dma_start(
                out=etab_o[ei * CAP + s5 * 128: ei * CAP + (s5 + 1) * 128, :],
                in_=eob[:])
            p.ST.dma(ins)
    p.ST.wait(nc.gpsimd)
    p.close()
    return p.nc




def build_C(final=False):
    p = Prog()
    nc = p.nc
    x0s = nc.dram_tensor("xs", [TLOC, D], F32, kind="ExternalInput").ap()
    decw = nc.dram_tensor("decw", [128, NTILE, 4], F32, kind="ExternalInput").ap()
    etab = nc.dram_tensor("etab", [E * CAP, D], BF16, kind="ExternalInput").ap()
    myq = nc.dram_tensor("myq", [1, 1], I32, kind="ExternalInput").ap()
    if final:
        x1_o = nc.dram_tensor("x3s", [TLOC, D], F32, kind="ExternalOutput").ap()
    else:
        x1_o = nc.dram_tensor("x1s", [TLOC, D], F32, kind="ExternalOutput").ap()
        ln2_o = nc.dram_tensor("ln2b", [TLOC, D], BF16, kind="ExternalOutput").ap()
    gscr = nc.dram_tensor("gscr", [128, 256], F32, kind="Internal").ap()

    scr = bookkeeping_scratch(p)
    gidx_f = p.sbuf("gidx_f", [128, 128], F32)
    gate2 = p.sbuf("gate2", [128, 128], F32)
    myq_sb = p.sbuf("myq_sb", [1, 2], I32)
    gidx_my_f = p.sbuf("gidx_my_f", [128, 32], F32)
    gidx_my = p.sbuf("gidx_my", [128, 32], I32)
    gate_my = p.sbuf("gate_my", [128, 32], F32)
    yk = [p.sbuf("yk0", [128, D], BF16), p.sbuf("yk1", [128, D], BF16)]
    ytmp = p.sbuf("ytmp", [128, D], F32)
    xt = p.sbuf("xt", [128, D], F32)
    stat = p.sbuf("stat", [128, 12], F32)
    mv = p.sbuf("mv", [128, 2], F32)
    sd = p.sbuf("sd", [128, 1], F32)
    rstd = p.sbuf("rstd", [128, 1], F32)
    lnf = p.sbuf("lnf", [128, D], F32)
    lnb = p.sbuf("lnbs", [128, D], BF16)

    pA = p.psum("pA", [128, 1024], F32)
    pB = p.psum("pB", [128, 1024], F32)
    pC = p.psum("pC", [128, 512], F32)

    p.block()
    p.sems()

    consts_sb = load_consts(p, ["cols8", "triu_bf", "striu_bf", "ones128x1_bf",
                                "ones1x128"])
    ins = nc.sync.dma_start(out=myq_sb[:, 0:1], in_=myq[:])
    p.LD.dma(ins)
    emit_bookkeeping(p, decw[:], scr, consts_sb, pA, pB, pC)

    vflag, slots, decsb = scr["vflag"], scr["slots"], scr["decsb"]
    p.DVE.wait(nc.vector)
    nc.vector.tensor_scalar_add(gidx_f[:], slots[:], -BIGIDX)
    ins = nc.vector.tensor_tensor(gidx_f[:], gidx_f[:], vflag[:], OP.mult)
    p.DVE.tick(ins)
    p.DVE.wait(nc.vector)
    ins = nc.vector.tensor_scalar_add(gidx_f[:], gidx_f[:], BIGIDX)
    p.DVE.tick(ins)
    nc.vector.tensor_copy(gate2[:].rearrange("p (j k) -> p j k", k=2),
                          decsb[:, :, 2:4])
    ins = nc.vector.tensor_tensor(gate2[:], gate2[:], vflag[:], OP.mult)
    p.DVE.tick(ins)
    p.DVE.wait(nc.sync)
    ins = nc.sync.dma_start(out=gscr[:, 0:128], in_=gidx_f[:])
    p.ST.dma(ins)
    ins = nc.sync.dma_start(out=gscr[:, 128:256], in_=gate2[:])
    p.ST.dma(ins)
    p.ST.wait(nc.sync)
    p.LD.wait(nc.sync)
    offreg = nc.sync.alloc_register("offr")
    nc.sync.load(offreg, myq_sb[0:1, 0:1])
    off = RuntimeValue(offreg)
    gidx_ap = bass.AP(gscr.tensor, off, [[256, 128], [1, 32]])
    gate_ap = bass.AP(gscr.tensor, off + 128, [[256, 128], [1, 32]])
    ins = nc.sync.dma_start(out=gidx_my_f[:], in_=gidx_ap)
    p.LD.dma(ins)
    ins = nc.sync.dma_start(out=gate_my[:], in_=gate_ap)
    p.LD.dma(ins)
    p.LD.wait(nc.vector)
    ins = nc.vector.tensor_copy(gidx_my[:], gidx_my_f[:])
    p.DVE.tick(ins)
    p.DVE.wait(nc.gpsimd)

    bcreg = nc.gpsimd.alloc_register("bcreg")
    nc.gpsimd.reg_mov(bcreg, E * CAP - 1)
    for j in range(NTILE_LOC):
        if j > 0:
            p.DVE.wait(nc.sync)
            p.ST.wait(nc.vector)
        ins = nc.sync.dma_start(out=xt[:], in_=x0s[j * 128:(j + 1) * 128, :])
        p.LD.dma(ins)
        for k in range(2):
            if j > 0:
                p.DVE.wait(nc.gpsimd)
            ins = nc.gpsimd.memset(yk[k][:], 0.0)
            p.GP.tick(ins)
            ins = nc.gpsimd.indirect_dma_start(
                out=yk[k][:, :], out_offset=None, in_=etab[:, :],
                in_offset=IndirectOffsetOnAxis(gidx_my[:, 2 * j + k:2 * j + k + 1],
                                               axis=0),
                bounds_check=bcreg, oob_is_err=False)
            p.GP.dma(ins)
        p.GP.wait(nc.vector)
        p.LD.wait(nc.vector)
        ins = nc.vector.tensor_scalar(ytmp[:], yk[0][:], gate_my[:, 2 * j:2 * j + 1],
                                      None, OP.mult)
        p.DVE.tick(ins)
        p.DVE.wait(nc.vector)
        ins = nc.vector.scalar_tensor_tensor(ytmp[:], yk[1][:],
                                             gate_my[:, 2 * j + 1:2 * j + 2],
                                             ytmp[:], OP.mult, OP.add)
        p.DVE.tick(ins)
        p.DVE.wait(nc.vector)
        ins = nc.vector.scalar_tensor_tensor(xt[:], ytmp[:], 0.5, xt[:],
                                             OP.mult, OP.add)
        p.DVE.tick(ins)
        p.DVE.wait(nc.sync)
        ins = nc.sync.dma_start(out=x1_o[j * 128:(j + 1) * 128, :], in_=xt[:])
        p.ST.dma(ins)
        if not final:
            emit_ln_tile(p, xt, stat, mv, sd, rstd, lnf, lnb)
            p.DVE.wait(nc.sync)
            ins = nc.sync.dma_start(out=ln2_o[j * 128:(j + 1) * 128, :], in_=lnb[:])
            p.ST.dma(ins)
    p.ST.wait(nc.gpsimd)
    p.close()
    return p.nc




NQC = T // 512
NKT = T // 128


def build_D(dbg=0, ntc=NQC, natt=None, epi=99):
    p = Prog()
    nc = p.nc
    ln2b = nc.dram_tensor("ln2b", [T, D], BF16, kind="ExternalInput").ap()
    wq4 = nc.dram_tensor("wq4", [D, 1024], F32, kind="ExternalInput").ap()
    gam = nc.dram_tensor("gam", [128, 512], F32, kind="ExternalInput").ap()
    bet = nc.dram_tensor("bet", [128, 512], F32, kind="ExternalInput").ap()
    gatedb = nc.dram_tensor("gatedb", [T, 256], BF16, kind="ExternalOutput").ap()
    if dbg == 6:
        dbg_done = nc.dram_tensor("dbg_done", [1, 16], F32, kind="ExternalOutput").ap()
        dbg_q = nc.dram_tensor("dbg_q", [128, 2, T], BF16, kind="ExternalOutput").ap()
        dbg_g = nc.dram_tensor("dbg_g", [128, NKT, 256], BF16, kind="ExternalOutput").ap()
        dbg_af = nc.dram_tensor("dbg_af", [128, 512], BF16, kind="ExternalOutput").ap()
        dbg_at = nc.dram_tensor("dbg_at", [128, 512], F32, kind="ExternalOutput").ap()
        dbg_nr = nc.dram_tensor("dbg_nr", [128, 512], F32, kind="ExternalOutput").ap()
        dbg_gg = nc.dram_tensor("dbg_gg", [128, 4, 128], BF16, kind="ExternalOutput").ap()
    if dbg and dbg != 6:
        dbg_q = nc.dram_tensor("dbg_q", [128, 2, T], BF16, kind="ExternalOutput").ap()
        dbg_k = nc.dram_tensor("dbg_k", [128, 2, T], BF16, kind="ExternalOutput").ap()
        dbg_v = nc.dram_tensor("dbg_v", [128, NKT, 256], BF16, kind="ExternalOutput").ap()
        dbg_g = nc.dram_tensor("dbg_g", [128, NKT, 256], BF16, kind="ExternalOutput").ap()
        dbg_done = nc.dram_tensor("dbg_done", [1, 16], F32, kind="ExternalOutput").ap()

    wq_sb = p.sbuf("wq_sb", [128, 8, 1024], BF16)
    lnF = [p.sbuf("lnF0", [128, 8, 512], BF16), p.sbuf("lnF1", [128, 8, 512], BF16)]
    q_sb = p.sbuf("q_sb", [128, 2, T], BF16)
    k_sb = p.sbuf("k_sb", [128, 2, T], BF16)
    v_sb = p.sbuf("v_sb", [128, NKT, 256], BF16)
    gate_sb = p.sbuf("gate_sb", [128, NKT, 256], BF16)
    w_sb = [p.sbuf("w_sb0", [128, 1024], BF16), p.sbuf("w_sb1", [128, 1024], BF16)]
    att_fm = p.sbuf("att_fm", [128, 512], BF16)
    att_tok = p.sbuf("att_tok", [128, 512], F32)
    sq_tok = p.sbuf("sq_tok", [128, 512], F32)
    ssum = p.sbuf("ssum", [128, 8], F32)
    ssq = p.sbuf("ssq", [128, 8], F32)
    mu = p.sbuf("mu", [128, 8], F32)
    var = p.sbuf("var", [128, 8], F32)
    rstd = p.sbuf("rstd", [128, 8], F32)
    mu2 = p.sbuf("mu2", [128, 8], F32)
    gam_sb = p.sbuf("gam_sb", [128, 512], F32)
    bet_sb = p.sbuf("bet_sb", [128, 512], F32)
    nrm = p.sbuf("nrm", [128, 512], F32)
    gated = p.sbuf("gated", [128, 4, 128], BF16)

    psS = [p.psum("psS0", [128, 1024], F32), p.psum("psS1", [128, 1024], F32)]
    psA = p.psum("psA", [128, 512], F32)
    psAo = p.psum("psAo", [128, 512], F32)
    psT = p.psum("psT", [128, 1024], BF16)
    psT2 = p.psum("psT2", [128, 1024], BF16)
    psV = psA

    p.block()
    p.sems()

    consts_sb = load_consts(p, ["eye64s_bf"])
    ins = nc.gpsimd.dma_start(out=wq_sb[:],
                              in_=wq4.rearrange("(c p) m -> p c m", p=128))
    p.GP.dma(ins)
    ins = nc.sync.dma_start(out=gam_sb[:], in_=gam[:])
    p.LD.dma(ins)
    ins = nc.sync.dma_start(out=bet_sb[:], in_=bet[:])
    p.LD.dma(ins)
    p.GP.wait(nc.tensor)

    qk_dst = {0: (q_sb, 0), 1: (q_sb, 1), 2: (k_sb, 0), 3: (k_sb, 1)}
    p.GP.wait(nc.sync)
    p.LD.wait(nc.sync)
    for tc in range(ntc):
        buf = lnF[tc % 2]
        p.PE.wait(nc.sync)
        for c in range(8):
            ins = nc.sync.dma_start_transpose(
                buf[:, c, :], ln2b[tc * 512:(tc + 1) * 512, c * 128:(c + 1) * 128])
            p.LD.dma(ins)
        p.LD.wait(nc.tensor)
        p.ACT.wait(nc.tensor)
        for mi in range(4):
            ps = psS[mi // 2]
            off = (mi % 2) * 512
            for c in range(8):
                ins = nc.tensor.matmul(ps[:, off:off + 512],
                                       wq_sb[:, c, mi * 128:(mi + 1) * 128],
                                       buf[:, c, :], start=(c == 0), stop=(c == 7))
        p.PE.tick(ins)
        p.PE.wait(nc.scalar)
        for mi in range(4):
            dst, pr = qk_dst[mi]
            ps = psS[mi // 2]
            off = (mi % 2) * 512
            ins = nc.scalar.activation(dst[:, pr, tc * 512:(tc + 1) * 512],
                                       ps[:, off:off + 512], AF.Silu)
        p.ACT.tick(ins)
        for sub in range(4):
            tkt = tc * 4 + sub
            if sub > 0:
                p.ACT.wait(nc.tensor)
            for c in range(8):
                nc.tensor.matmul(psV[:, 0:256], buf[:, c, sub * 128:(sub + 1) * 128],
                                 wq_sb[:, c, 512:768], start=(c == 0), stop=(c == 7))
            for c in range(8):
                ins = nc.tensor.matmul(psV[:, 256:512],
                                       buf[:, c, sub * 128:(sub + 1) * 128],
                                       wq_sb[:, c, 768:1024], start=(c == 0), stop=(c == 7))
            p.PE.tick(ins)
            p.PE.wait(nc.scalar)
            nc.scalar.activation(gate_sb[:, tkt, :], psV[:, 0:256], AF.Silu)
            ins = nc.scalar.activation(v_sb[:, tkt, :], psV[:, 256:512], AF.Silu)
            p.ACT.tick(ins)

    if dbg and dbg != 6:
        p.ACT.wait(nc.sync)
        for (dst, srcb) in ((dbg_q, q_sb), (dbg_k, k_sb), (dbg_v, v_sb), (dbg_g, gate_sb)):
            ins = nc.sync.dma_start(out=dst, in_=srcb[:])
            p.ST.dma(ins)

    SCALE = 1.0 / 8.0
    attiters = []
    for pair in range(2):
        for qc in range(NQC):
            attiters.append((pair, qc))
    if dbg == 2:
        attiters = []
    if natt is not None:
        attiters = attiters[:natt]
    for (pair, qc) in attiters:
        if True:
            silu_acts = []
            av_pes = []
            for kt in range(NKT):
                par = kt % 2
                ps = psS[par]
                if kt >= 2:
                    p.ACT.wait_val(nc.tensor, silu_acts[kt - 2])
                else:
                    p.ACT.wait(nc.tensor)
                nc.tensor.matmul(ps[:, 0:512], k_sb[0:64, pair, kt * 128:(kt + 1) * 128],
                                 q_sb[0:64, pair, qc * 512:(qc + 1) * 512],
                                 start=True, stop=True, tile_position=(0, 0))
                ins = nc.tensor.matmul(ps[:, 512:1024],
                                       k_sb[64:128, pair, kt * 128:(kt + 1) * 128],
                                       q_sb[64:128, pair, qc * 512:(qc + 1) * 512],
                                       start=True, stop=True, tile_position=(64, 0))
                p.PE.tick(ins)
                p.PE.wait(nc.scalar)
                if kt >= 2:
                    pass
                ins = nc.scalar.activation(w_sb[par][:], ps[:, :], AF.Silu, scale=SCALE)
                p.ACT.tick(ins)
                silu_acts.append(p.ACT.v)
                p.ACT.wait_val(nc.tensor, silu_acts[kt])
                nc.tensor.matmul(psA[0:64, :], v_sb[:, kt, pair * 128:pair * 128 + 64],
                                 w_sb[par][:, 0:512], start=(kt == 0), stop=(kt == NKT - 1),
                                 tile_position=(0, 0))
                ins = nc.tensor.matmul(psAo[64:128, :],
                                       v_sb[:, kt, pair * 128 + 64:pair * 128 + 128],
                                       w_sb[par][:, 512:1024], start=(kt == 0),
                                       stop=(kt == NKT - 1), tile_position=(0, 64))
                p.PE.tick(ins)
                av_pes.append(p.PE.v)

            p.PE.wait(nc.scalar)
            nc.scalar.copy(att_fm[0:64, :], psA[0:64, :])
            ins = nc.scalar.copy(att_fm[64:128, :], psAo[64:128, :])
            p.ACT.tick(ins)
            if epi == 1:
                continue
            p.ACT.wait(nc.tensor)
            for sub in range(4):
                for hh in range(2):
                    dstp = psT if hh == 0 else psT2
                    ins = nc.tensor.transpose(
                        dstp[:, sub * 64:(sub + 1) * 64],
                        att_fm[hh * 64:(hh + 1) * 64, sub * 128:(sub + 1) * 128],
                        consts_sb["eye64s_bf"][hh * 64:(hh + 1) * 64, :])
            p.PE.tick(ins)
            if epi == 2:
                continue
            p.PE.wait(nc.scalar)
            nc.scalar.copy(
                att_tok[:].rearrange("p (s h d) -> p s h d", s=4, h=2)[:, :, 0, :],
                psT[:, 0:256].rearrange("p (s d) -> p s d", s=4))
            ins = nc.scalar.copy(
                att_tok[:].rearrange("p (s h d) -> p s h d", s=4, h=2)[:, :, 1, :],
                psT2[:, 0:256].rearrange("p (s d) -> p s d", s=4))
            p.ACT.tick(ins)
            if epi == 3:
                continue
            p.ACT.wait(nc.scalar)
            ins = nc.scalar.activation(sq_tok[:], att_tok[:], AF.Square)
            p.ACT.tick(ins)
            p.ACT.wait(nc.vector)
            nc.vector.reduce_sum(ssum[:], att_tok[:].rearrange("p (s e) -> p s e", e=64),
                                 axis=mybir.AxisListType.X)
            ins = nc.vector.reduce_sum(ssq[:], sq_tok[:].rearrange("p (s e) -> p s e", e=64),
                                       axis=mybir.AxisListType.X)
            p.DVE.tick(ins)
            if epi == 4:
                continue
            p.DVE.wait(nc.vector)
            nc.vector.tensor_scalar(mu[:], ssum[:], 1.0 / HD, None, OP.mult)
            ins = nc.vector.tensor_scalar(var[:], ssq[:], 1.0 / HD, None, OP.mult)
            p.DVE.tick(ins)
            p.DVE.wait(nc.vector)
            ins = nc.vector.tensor_tensor(mu2[:], mu[:], mu[:], OP.mult)
            p.DVE.tick(ins)
            p.DVE.wait(nc.vector)
            ins = nc.vector.tensor_tensor(var[:], var[:], mu2[:], OP.subtract)
            p.DVE.tick(ins)
            p.DVE.wait(nc.vector)
            ins = nc.vector.tensor_scalar_add(var[:], var[:], EPS)
            p.DVE.tick(ins)
            p.DVE.wait(nc.scalar)
            ins = nc.scalar.activation(var[:], var[:], AF.Sqrt)
            p.ACT.tick(ins)
            p.ACT.wait(nc.vector)
            ins = nc.vector.reciprocal(rstd[:], var[:])
            p.DVE.tick(ins)
            p.DVE.wait(nc.vector)
            for c8 in range(8):
                ins = nc.vector.tensor_scalar(nrm[:, c8 * 64:(c8 + 1) * 64],
                                              att_tok[:, c8 * 64:(c8 + 1) * 64],
                                              mu[:, c8:c8 + 1], rstd[:, c8:c8 + 1],
                                              OP.subtract, OP.mult)
            p.DVE.tick(ins)
            p.DVE.wait(nc.vector)
            ins = nc.vector.tensor_tensor(nrm[:], nrm[:], gam_sb[:], OP.mult)
            p.DVE.tick(ins)
            p.DVE.wait(nc.vector)
            ins = nc.vector.tensor_tensor(nrm[:], nrm[:], bet_sb[:], OP.add)
            p.DVE.tick(ins)
            p.DVE.wait(nc.vector)
            p.ST.wait(nc.vector)
            ins = nc.vector.tensor_tensor(
                gated[:], nrm[:].rearrange("p (s c) -> p s c", c=128),
                gate_sb[:, qc * 4:qc * 4 + 4, pair * 128:(pair + 1) * 128], OP.mult)
            p.DVE.tick(ins)
            p.DVE.wait(nc.sync)
            ins = nc.sync.dma_start(
                out=gatedb[qc * 512:(qc + 1) * 512, pair * 128:(pair + 1) * 128]
                .rearrange("(s p) c -> p s c", p=128),
                in_=gated[:])
            p.ST.dma(ins)
    if dbg == 6:
        p.DVE.wait(nc.sync)
        p.ACT.wait(nc.sync)
        for (d2, s2) in ((dbg_q, q_sb[:]), (dbg_g, gate_sb[:]), (dbg_af, att_fm[:]),
                         (dbg_at, att_tok[:]), (dbg_nr, nrm[:]),
                         (dbg_gg, gated[:])):
            ins = nc.sync.dma_start(out=d2, in_=s2)
            p.ST.dma(ins)
    if dbg:
        one = p.sbuf("one_sb", [1, 16], F32)
        ins = nc.vector.memset(one[:], 7.0)
        p.DVE.tick(ins)
        p.DVE.wait(nc.sync)
        ins = nc.sync.dma_start(out=dbg_done, in_=one[:])
        p.ST.dma(ins)
    p.ST.wait(nc.gpsimd)
    p.close()
    return p.nc




def build_E():
    p = Prog()
    nc = p.nc
    x1s = nc.dram_tensor("x1s", [TLOC, D], F32, kind="ExternalInput").ap()
    gfull = nc.dram_tensor("gfull", [TLOC, D], BF16, kind="ExternalInput").ap()
    wo = nc.dram_tensor("wo", [D, D], F32, kind="ExternalInput").ap()
    wr = nc.dram_tensor("wr", [D, E], F32, kind="ExternalInput").ap()
    ident = nc.dram_tensor("ident", [128, 128], F32, kind="ExternalInput").ap()
    x2_o = nc.dram_tensor("x2s", [TLOC, D], F32, kind="ExternalOutput").ap()
    ln3_o = nc.dram_tensor("ln3b", [TLOC, D], BF16, kind="ExternalOutput").ap()
    dec_o = nc.dram_tensor("dec", [TLOC, 4], F32, kind="ExternalOutput").ap()

    wo_sb = p.sbuf("wo_sb", [128, 8, D], BF16)
    gfT = [p.sbuf("gfT0", [128, 8, 128], BF16), p.sbuf("gfT1", [128, 8, 128], BF16)]
    xt = p.sbuf("xt", [128, D], F32)
    stat = p.sbuf("stat", [128, 12], F32)
    mv = p.sbuf("mv", [128, 2], F32)
    sd = p.sbuf("sd", [128, 1], F32)
    rstd = p.sbuf("rstd", [128, 1], F32)
    lnf = p.sbuf("lnf", [128, D], F32)
    lnb = p.sbuf("lnbs", [128, D], BF16)
    lnT = p.sbuf("lnT", [128, D], F32)
    identsb = p.sbuf("identsb", [128, 128], F32)
    wr_sb = p.sbuf("wr_sb", [128, 8, E], F32)
    logit = p.sbuf("logit", [128, E], F32)
    scr = {
        "mx1": p.sbuf("mx1", [128, 1], F32),
        "sm": p.sbuf("sm", [128, 1], F32),
        "mx8": p.sbuf("mx8", [128, 8], F32),
        "mi8": p.sbuf("mi8", [128, 8], U32),
        "mif": p.sbuf("mif", [128, 8], F32),
    }
    dec = p.sbuf("dec_sb", [128, 4], F32)
    psO = [p.psum("psO0", [128, 512], F32), p.psum("psO1", [128, 512], F32)]
    ptr = [p.psum("ptr0", [128, 512], F32), p.psum("ptr1", [128, 512], F32)]
    plog = p.psum("plog", [128, 512], F32)

    p.block()
    p.sems()

    ins = nc.sync.dma_start(out=identsb[:], in_=ident[:])
    p.LD.dma(ins)
    ins = nc.sync.dma_start(out=wr_sb[:], in_=wr.rearrange("(c p) e -> p c e", p=128))
    p.LD.dma(ins)
    ins = nc.gpsimd.dma_start(out=wo_sb[:], in_=wo.rearrange("(kc p) n -> p kc n", p=128))
    p.GP.dma(ins)
    p.GP.wait(nc.tensor)
    p.GP.wait(nc.sync)

    for j in range(NTILE_LOC):
        buf = gfT[j % 2]
        p.PE.wait(nc.sync)
        for c in range(8):
            ins = nc.sync.dma_start_transpose(
                buf[:, c, :], gfull[j * 128:(j + 1) * 128, c * 128:(c + 1) * 128])
            p.LD.dma(ins)
        if j > 0:
            p.DVE.wait(nc.sync)
            p.ST.wait(nc.vector)
        ins = nc.sync.dma_start(out=xt[:], in_=x1s[j * 128:(j + 1) * 128, :])
        p.LD.dma(ins)
        p.LD.wait(nc.tensor)
        p.DVE.wait(nc.tensor)
        for n2 in range(2):
            for kc in range(8):
                ins = nc.tensor.matmul(psO[n2][:, :], buf[:, kc, :],
                                       wo_sb[:, kc, n2 * 512:(n2 + 1) * 512],
                                       start=(kc == 0), stop=(kc == 7))
        p.PE.tick(ins)
        p.PE.wait(nc.vector)
        p.LD.wait(nc.vector)
        for n2 in range(2):
            ins = nc.vector.tensor_add(xt[:, n2 * 512:(n2 + 1) * 512],
                                       xt[:, n2 * 512:(n2 + 1) * 512], psO[n2][:, :])
        p.DVE.tick(ins)
        p.DVE.wait(nc.sync)
        ins = nc.sync.dma_start(out=x2_o[j * 128:(j + 1) * 128, :], in_=xt[:])
        p.ST.dma(ins)
        emit_ln_tile(p, xt, stat, mv, sd, rstd, lnf, lnb)
        p.DVE.wait(nc.sync)
        ins = nc.sync.dma_start(out=ln3_o[j * 128:(j + 1) * 128, :], in_=lnb[:])
        p.ST.dma(ins)
        p.DVE.wait(nc.tensor)
        emit_router_tile(p, lnf, identsb, wr_sb, lnT,
                         [ptr[0][:, 0:128], ptr[1][:, 0:128]], plog[:, 0:8],
                         logit, scr, dec)
        p.DVE.wait(nc.sync)
        ins = nc.sync.dma_start(out=dec_o[j * 128:(j + 1) * 128, :], in_=dec[:])
        p.ST.dma(ins)
        p.ST.wait(nc.sync)
    p.ST.wait(nc.gpsimd)
    p.close()
    return p.nc



def build_A():
    p = Prog()
    nc = p.nc
    x0s = nc.dram_tensor("x0s", [TLOC, D], F32, kind="ExternalInput").ap()
    wr = nc.dram_tensor("wr", [D, E], F32, kind="ExternalInput").ap()
    ident = nc.dram_tensor("ident", [128, 128], F32, kind="ExternalInput").ap()
    lnb_o = nc.dram_tensor("lnb", [TLOC, D], BF16, kind="ExternalOutput").ap()
    dec_o = nc.dram_tensor("dec", [TLOC, 4], F32, kind="ExternalOutput").ap()

    xt = p.sbuf("xt", [128, D], F32)
    stat = p.sbuf("stat", [128, 12], F32)
    mv = p.sbuf("mv", [128, 2], F32)
    sd = p.sbuf("sd", [128, 1], F32)
    rstd = p.sbuf("rstd", [128, 1], F32)
    lnf = p.sbuf("lnf", [128, D], F32)
    lnb = p.sbuf("lnbs", [128, D], BF16)
    lnT = p.sbuf("lnT", [128, D], F32)
    identsb = p.sbuf("identsb", [128, 128], F32)
    wr_sb = p.sbuf("wr_sb", [128, 8, E], F32)
    logit = p.sbuf("logit", [128, E], F32)
    scr = {
        "mx1": p.sbuf("mx1", [128, 1], F32),
        "sm": p.sbuf("sm", [128, 1], F32),
        "mx8": p.sbuf("mx8", [128, 8], F32),
        "mi8": p.sbuf("mi8", [128, 8], U32),
        "mif": p.sbuf("mif", [128, 8], F32),
    }
    dec = p.sbuf("dec_sb", [128, 4], F32)
    ptr = [p.psum("ptr0", [128, 128], F32), p.psum("ptr1", [128, 128], F32)]
    plog = p.psum("plog", [128, E], F32)

    p.block()
    p.sems()

    ins = nc.sync.dma_start(out=identsb[:], in_=ident[:])
    p.LD.dma(ins)
    ins = nc.sync.dma_start(out=wr_sb[:], in_=wr.rearrange("(c p) e -> p c e", p=128))
    p.LD.dma(ins)

    for i in range(NTILE_LOC):
        if i > 0:
            p.DVE.wait(nc.sync)
            p.ST.wait(nc.vector)
        ins = nc.sync.dma_start(out=xt[:], in_=x0s[i * 128:(i + 1) * 128, :])
        p.LD.dma(ins)
        p.LD.wait(nc.vector)
        if i > 0:
            p.PE.wait(nc.vector)
        emit_ln_tile(p, xt, stat, mv, sd, rstd, lnf, lnb)
        p.DVE.wait(nc.sync)
        ins = nc.sync.dma_start(out=lnb_o[i * 128:(i + 1) * 128, :], in_=lnb[:])
        p.ST.dma(ins)
        p.DVE.wait(nc.tensor)
        emit_router_tile(p, lnf, identsb, wr_sb, lnT, ptr, plog, logit, scr, dec)
        p.DVE.wait(nc.sync)
        ins = nc.sync.dma_start(out=dec_o[i * 128:(i + 1) * 128, :], in_=dec[:])
        p.ST.dma(ins)
        p.ST.wait(nc.sync)
    p.ST.wait(nc.gpsimd)
    p.close()
    return p.nc



_CACHE = {}


def _get(name, builder):
    if name not in _CACHE:
        _CACHE[name] = builder()
    return _CACHE[name]


def kernel(L0_attn_hidden_states, L0_attn_w_qkuv, L0_attn_w_out,
           L0_attn_ln_gamma, L0_attn_ln_beta, L0_ffn_w_router,
           L0_ffn_wi, L0_ffn_wo):
    x0 = np.ascontiguousarray(np.asarray(L0_attn_hidden_states, dtype=np.float32)
                              ).reshape(GG, T, D)
    wq = np.ascontiguousarray(np.asarray(L0_attn_w_qkuv, dtype=np.float32))
    w_out = np.ascontiguousarray(np.asarray(L0_attn_w_out, dtype=np.float32))
    gamma = np.asarray(L0_attn_ln_gamma, dtype=np.float32)
    beta = np.asarray(L0_attn_ln_beta, dtype=np.float32)
    wr = np.ascontiguousarray(np.asarray(L0_ffn_w_router, dtype=np.float32))
    wi = np.ascontiguousarray(np.asarray(L0_ffn_wi, dtype=np.float32))
    wo = np.ascontiguousarray(np.asarray(L0_ffn_wo, dtype=np.float32))

    consts = host_consts()
    cns = {"c_" + n: consts[n] for n in
           ("cols8", "triu_bf", "striu_bf", "ones128x1_bf", "ones1x128")}
    cores = list(range(8))
    gq = [divmod(c, QP) for c in cores]

    def decw_of(dec):
        return [np.ascontiguousarray(
            dec[g].reshape(NTILE, 128, 4).transpose(1, 0, 2)) for g in range(GG)]

    def gather_full(res, key, axis=0):
        return [np.concatenate([np.asarray(res.results[g * QP + q][key])
                                for q in range(QP)], axis=axis) for g in range(GG)]

    def per_core(res, key):
        return [np.asarray(r[key]) for r in res.results]

    resA = run_bass_kernel_spmd(_get("A", build_A), [
        {"x0s": np.ascontiguousarray(x0[g, q * TLOC:(q + 1) * TLOC]),
         "wr": wr, "ident": consts["ident"]}
        for (g, q) in gq], core_ids=cores)
    lnb1 = gather_full(resA, "lnb")
    dec1 = gather_full(resA, "dec")
    decw1 = decw_of(dec1)

    def expert_maps(lnb, decw):
        return [
            {**cns, "lnb": np.ascontiguousarray(lnb[g]),
             "decw": decw[g],
             "wi2": wi[2 * q:2 * q + 2], "wo2": wo[2 * q:2 * q + 2],
             "eq0": np.tile(np.array([2.0 * q, 2.0 * q + 1], dtype=np.float32),
                            (128, 1))}
            for (g, q) in gq]
    resB = run_bass_kernel_spmd(_get("B", build_B), expert_maps(lnb1, decw1),
                                core_ids=cores)
    etab1 = gather_full(resB, "etab")

    resC = run_bass_kernel_spmd(_get("C", build_C), [
        {**cns, "xs": np.ascontiguousarray(x0[g, q * TLOC:(q + 1) * TLOC]),
         "decw": decw1[g], "etab": etab1[g],
         "myq": np.array([[32 * q]], dtype=np.int32)}
        for (g, q) in gq], core_ids=cores)
    x1s = per_core(resC, "x1s")
    ln2 = gather_full(resC, "ln2b")

    din = []
    for (g, q) in gq:
        cols = []
        for pair in range(2):
            h0 = 4 * q + 2 * pair
            cols += list(range(h0 * 64, (h0 + 2) * 64))
        for pair in range(2):
            h0 = 4 * q + 2 * pair
            cols += list(range(1024 + h0 * 64, 1024 + (h0 + 2) * 64))
        h0 = 4 * q
        cols += list(range(3072 + h0 * 64, 3072 + (h0 + 4) * 64))
        cols += list(range(2048 + h0 * 64, 2048 + (h0 + 4) * 64))
        din.append({
            "c_eye64s_bf": consts["eye64s_bf"],
            "ln2b": np.ascontiguousarray(ln2[g]),
            "wq4": np.ascontiguousarray(wq[:, cols]),
            "gam": np.ascontiguousarray(np.tile(gamma, (128, 8))),
            "bet": np.ascontiguousarray(np.tile(beta, (128, 8)))})
    resD = run_bass_kernel_spmd(_get("D", build_D), din, core_ids=cores)
    gated = gather_full(resD, "gatedb", axis=1)

    resE = run_bass_kernel_spmd(_get("E", build_E), [
        {"x1s": x1s[c], "gfull": np.ascontiguousarray(
            gated[g][q * TLOC:(q + 1) * TLOC]),
         "wo": w_out, "wr": wr, "ident": consts["ident"]}
        for c, (g, q) in enumerate(gq)], core_ids=cores)
    x2s = per_core(resE, "x2s")
    ln3 = gather_full(resE, "ln3b")
    dec3 = gather_full(resE, "dec")
    decw3 = decw_of(dec3)

    resF = run_bass_kernel_spmd(_get("B", build_B), expert_maps(ln3, decw3),
                                core_ids=cores)
    etab3 = gather_full(resF, "etab")

    resG = run_bass_kernel_spmd(_get("G", lambda: build_C(final=True)), [
        {**cns, "xs": x2s[c], "decw": decw3[g], "etab": etab3[g],
         "myq": np.array([[32 * q]], dtype=np.int32)}
        for c, (g, q) in enumerate(gq)], core_ids=cores)
    x3 = gather_full(resG, "x3s")
    out = np.stack(x3).reshape(GG, T, D).astype(np.float32)
    return out
